# version 27
# baseline (speedup 1.0000x reference)
import numpy as np
import concourse.bass as bass
import concourse.mybir as mybir
from concourse.bass_utils import run_bass_kernel_spmd

F32 = mybir.dt.float32
BF16 = mybir.dt.bfloat16
AF = mybir.ActivationFunctionType
ALU = mybir.AluOpType
AXX = mybir.AxisListType.X

D = 1024
FF = 4096
HCN = 8
FCN = 32
TS = 512
T_TOK = 8192
NCH = T_TOK // TS
NEG_INIT = -1e5
EPS = 1e-5
W1SL = 16
INV_D = 1.0 / D

ENGINES = ("sync", "scalar", "vector", "tensor", "gpsimd")
DMA_SEM = {"sync": "WD", "gpsimd": "XD", "scalar": "AD"}
ENG_SEM = {"tensor": "TEN", "vector": "VEC", "scalar": "ACT", "gpsimd": "GPS", "sync": "SYN"}


def _rng(ap):
    dts = mybir.dt.size(ap.dtype)
    aps = [list(p) for p in ap.ap]
    try:
        is_dram = str(ap.space) == "MemorySpace.DRAM"
    except Exception:
        is_dram = False
    dims = aps if is_dram else aps[1:]
    span = (sum(abs(s) * (c - 1) for s, c in dims) + 1) if dims else 1
    lo = ap.offset * dts
    return (ap.tensor.name, lo, lo + span * dts)


class _Op:
    __slots__ = ("eng", "fn", "reads", "writes", "dma", "qsem", "deps", "pub", "val", "idx")

    def __init__(self, eng, fn, reads, writes, dma, idx, qsem=None):
        self.eng, self.fn = eng, fn
        self.reads = [_rng(a) for a in reads]
        self.writes = [_rng(a) for a in writes]
        self.dma = dma
        self.qsem = qsem
        self.deps = []
        self.pub = dma
        self.val = None
        self.idx = idx


class Sched:
    def __init__(self):
        self.ops = []
        self.mem = {}
        self.last_dma = {}
        self.n_stream = {}

    def op(self, eng, fn, reads=(), writes=(), dma=False, stream=False):
        qsem = None
        if dma:
            if stream:
                k = self.n_stream.setdefault(eng, 0)
                self.n_stream[eng] = k + 1
                qsem = f"W{({'sync': 0, 'scalar': 2, 'gpsimd': 4}[eng]) + k % 2}"
            else:
                qsem = DMA_SEM[eng]
        o = _Op(eng, fn, reads, writes, dma, len(self.ops), qsem)
        self.ops.append(o)
        dep = set()
        if dma:
            hist = self.last_dma.setdefault((eng, qsem), [])
            if hist:
                dep.add(hist[-1])
            hist.append(o)

        def scan_name(name, lo, hi, want_writers, want_readers):
            ent = self.mem.get(name)
            if not ent:
                return
            for (l2, h2), (wr, rds) in ent.items():
                if l2 < hi and lo < h2:
                    if want_writers and wr is not None:
                        dep.add(wr)
                    if want_readers:
                        for r in rds:
                            dep.add(r)

        for (name, lo, hi) in o.reads:
            scan_name(name, lo, hi, True, False)
        for (name, lo, hi) in o.writes:
            scan_name(name, lo, hi, True, True)
        for (name, lo, hi) in o.reads:
            ent = self.mem.setdefault(name, {})
            ent.setdefault((lo, hi), [None, []])[1].append(o)
        for (name, lo, hi) in o.writes:
            ent = self.mem.setdefault(name, {})
            e = ent.get((lo, hi))
            if e is not None:
                e[0], e[1] = o, []
            else:
                ent[(lo, hi)] = [o, []]
        for p in dep:
            if p is o:
                continue
            if p.eng == o.eng and not p.dma and not o.dma:
                continue
            o.deps.append(p)
        return o

    def emit(self, nc, sems):
        for o in self.ops:
            if not o.deps:
                continue
            best = {}
            for p in o.deps:
                key = p.qsem if p.dma else ENG_SEM[p.eng]
                q = best.get(key)
                if q is None or p.idx > q.idx:
                    best[key] = p
            o.deps = list(best.values())
            for p in o.deps:
                p.pub = True
        cnt = {k: 0 for k in sems}
        for o in self.ops:
            if o.dma:
                o.pub = True
            if o.pub:
                s = o.qsem if o.dma else ENG_SEM[o.eng]
                cnt[s] += 16 if o.dma else 1
                o.val = (s, cnt[s])
        prog = {e: [] for e in ENGINES}
        water = {e: {} for e in ENGINES}
        dirty_w = {e: [] for e in ENGINES}
        dirty_r = {e: [] for e in ENGINES}

        def overlap(r, ds):
            for (n2, l2, h2) in ds:
                if r[0] == n2 and r[1] < h2 and l2 < r[2]:
                    return True
            return False

        for o in self.ops:
            e = o.eng
            for p in sorted(o.deps, key=lambda q: q.idx):
                s, v = p.val
                if water[e].get(s, 0) < v:
                    water[e][s] = v
                    prog[e].append(lambda eng, s=s, v=v: eng.wait_ge(sems[s], v))
            if e in ("vector", "scalar", "gpsimd") and not o.dma:
                need = (any(overlap(r, dirty_w[e]) for r in o.reads)
                        or any(overlap(w, dirty_w[e]) or overlap(w, dirty_r[e]) for w in o.writes))
                if need:
                    prog[e].append(lambda eng: eng.drain())
                    dirty_w[e] = []
                    dirty_r[e] = []
                dirty_w[e].extend(o.writes)
                dirty_r[e].extend(o.reads)
            if o.val is not None:
                s, _ = o.val
                k = 16 if o.dma else 1
                prog[e].append(lambda eng, fn=o.fn, s=s, k=k: fn(eng).then_inc(sems[s], k))
            else:
                prog[e].append(lambda eng, fn=o.fn: fn(eng))
        return prog


def build_nc(n_tok=T_TOK):
    nch = n_tok // TS
    nc = bass.Bass("TRN2", target_bir_lowering=False)

    DT = lambda name, shape, dt=F32, kind="ExternalInput": nc.dram_tensor(name, shape, dt, kind=kind).ap()
    x_d = DT("x", [D, n_tok])
    y_d = DT("y", [D, n_tok], kind="ExternalOutput")
    wk_d = DT("wk", [D, D]); wv_d = DT("wv", [D, D]); wr_d = DT("wr", [D, D]); wo_d = DT("wo", [D, D])
    w1t_d = DT("w1t", [D, FF])
    w2t_d = DT("w2t", [FF, D])
    b1_d = DT("b1v", [128, FCN])
    b2_d = DT("b2v", [128, HCN])
    td_d = DT("tdv", [128, HCN]); tf_d = DT("tfv", [128, HCN])
    tmk_d = DT("tmkv", [128, HCN]); tmv_d = DT("tmvv", [128, HCN]); tmr_d = DT("tmrv", [128, HCN])
    sw1_d = nc.dram_tensor("sw1", [W1SL, 128, HCN, FF // W1SL], BF16, kind="Internal").ap()
    sw2_d = nc.dram_tensor("sw2", [2 * HCN, 128, FCN // 2, 128], BF16, kind="Internal").ap()
    satt_d = nc.dram_tensor("satt", [4, HCN, 128, 2, HCN, 128], BF16, kind="Internal").ap()

    SB = lambda name, shape, dt=F32: nc.alloc_sbuf_tensor(name, shape, dt).ap()
    xb = SB("xb", [128, HCN, TS])
    xb2 = SB("xb2", [128, HCN, TS])
    scr = SB("scr", [128, HCN, TS])
    lnh = SB("lnh", [128, HCN, TS], BF16)
    lnl = SB("lnl", [128, HCN, TS], BF16)
    arena = SB("arena", [128, HCN, 4 * TS])
    SLOT = [arena[:, :, i * TS:(i + 1) * TS] for i in range(4)]
    aflat = arena.rearrange("p h t -> p (h t)")
    aren_bf = aflat.bitcast(BF16)
    hhl = aren_bf.rearrange("p (n t) -> p n t", t=TS)
    hh = hhl[:, 0:FCN, :]
    hl = hhl[:, FCN:2 * FCN, :]
    ohb = SLOT[0].bitcast(BF16)
    oh = ohb[:, :, 0:TS]
    ol = ohb[:, :, TS:2 * TS]
    stg32 = aflat[:, 0:4096]
    stgsc = aflat[:, 4096:8192]
    stghi = aren_bf[:, 16384:20480]
    stglo = aren_bf[:, 20480:24576]
    sttrn = aren_bf[:, 24576:28672]

    edall = SB("edall", [128, HCN, TS])
    mb = SB("mb", [128, TS]); vva = SB("vva", [128, TS])
    rstd = vva
    hF = SB("hF", [128, 2, TS])
    stmp = SB("stmp", [128, 2, TS])
    w1b = SB("w1b", [128, 4, HCN, FF // W1SL], BF16)
    w2b = SB("w2b", [128, 4, FCN // 2, 128], BF16)
    wab = SB("wab", [128, 4, 2, HCN, 128], BF16)
    ones128 = SB("ones128", [128, 128])
    b1s = SB("b1s", [128, FCN])
    b2h = SB("b2h", [128, HCN])
    tmx = SB("tmx", [128, 3, HCN])
    etf = SB("etf", [128, HCN])
    ed8 = SB("ed8", [128, HCN])
    tdn = SB("tdn", [128, HCN])
    SNc = SB("SNc", [128, HCN]); SDc = SB("SDc", [128, HCN]); Mc = SB("Mc", [128, HCN])
    kmax = SB("kmax", [128, HCN]); Mx = SB("Mx", [128, HCN]); negM = SB("negM", [128, HCN])
    cft = SB("cft", [128, HCN]); N0 = SB("N0", [128, HCN]); D0 = SB("D0", [128, HCN])
    ucar = SB("ucar", [128, HCN, 1])
    wsc = SB("wsc", [128, 4])
    cst = SB("cst", [128, 2])
    absa = SB("absa", [128, FCN])

    PS = lambda name: nc.alloc_psum_tensor(name, [128, TS], F32).ap()
    pp = [PS("pp0"), PS("pp1"), PS("pp2"), PS("pp3")]
    po = [PS("po0"), PS("po1")]
    pst = [PS("pst0"), PS("pst1")]

    sems = {k: nc.alloc_semaphore(k) for k in ("WD", "XD", "AD", "W0", "W1", "W2", "W3", "W4", "W5", "TEN", "VEC", "ACT", "GPS", "SYN")}

    S = Sched()
    V = lambda fn, r, w: S.op("vector", fn, r, w)
    Sc = lambda fn, r, w: S.op("scalar", fn, r, w)
    T = lambda fn, r, w: S.op("tensor", fn, r, w)
    G = lambda fn, r, w: S.op("gpsimd", fn, r, w)
    DS = lambda dst, src: S.op("sync", lambda e, d=dst, s=src: e.dma_start(out=d, in_=s), [src], [dst], dma=True)
    DSW = lambda dst, src: S.op("sync", lambda e, d=dst, s=src: e.dma_start(out=d, in_=s), [src], [dst], dma=True, stream=True)
    DGW = lambda dst, src: S.op("gpsimd", lambda e, d=dst, s=src: e.dma_start(out=d, in_=s), [src], [dst], dma=True, stream=True)
    DG = lambda dst, src: S.op("gpsimd", lambda e, d=dst, s=src: e.dma_start(out=d, in_=s), [src], [dst], dma=True)
    DA = lambda dst, src: S.op("scalar", lambda e, d=dst, s=src: e.dma_start(out=d, in_=s), [src], [dst], dma=True)
    DAS = lambda dst, src: S.op("scalar", lambda e, d=dst, s=src: e.dma_start(out=d, in_=s), [src], [dst], dma=True, stream=True)
    DW = lambda idx, dst, src: DSW(dst, src)
    DWA = lambda t, dst, src: (DGW if t % 2 == 0 else DAS)(dst, src)
    VG = lambda i, fn, r, w: (V if (i % HCN) < 5 else G)(fn, r, w)

    DG(b1s[:, :], b1_d[:, :])
    DG(b2h[:, :], b2_d[:, :])
    DG(tmx[:, 0, :], tmk_d[:, :])
    DG(tmx[:, 1, :], tmv_d[:, :])
    DG(tmx[:, 2, :], tmr_d[:, :])
    DG(etf[:, :], tf_d[:, :])
    DG(ed8[:, :], td_d[:, :])
    V(lambda e: e.memset(ones128[:, :], 1.0), [], [ones128])
    V(lambda e: e.memset(SNc[:, :], 0.0), [], [SNc])
    V(lambda e: e.memset(SDc[:, :], 0.0), [], [SDc])
    V(lambda e: e.memset(Mc[:, :], NEG_INIT), [], [Mc])
    V(lambda e: e.memset(ucar[:, :, :], 0.0), [], [ucar])
    V(lambda e: e.memset(cst[:, 0:1], EPS), [], [cst[:, 0:1]])
    V(lambda e: e.memset(cst[:, 1:2], 1.0), [], [cst[:, 1:2]])
    V(lambda e: e.tensor_scalar(b2h[:, :], b2h[:, :], 0.5, None, ALU.mult), [b2h], [b2h])
    Sc(lambda e: e.activation(etf[:, :], etf[:, :], AF.Exp), [etf], [etf])
    Sc(lambda e: e.activation(tdn[:, :], ed8[:, :], AF.Exp), [ed8], [tdn])
    V(lambda e: e.tensor_scalar(tdn[:, :], tdn[:, :], -1.0, None, ALU.mult), [tdn], [tdn])
    V(lambda e: e.memset(mb[:, :], 0.0), [], [mb])
    for hc in range(HCN):
        Sc(lambda e, hc=hc: e.activation(edall[:, hc, :], mb[:, :], AF.Exp, bias=tdn[:, hc:hc + 1],
                                         scale=0.0),
           [mb, tdn[:, hc:hc + 1]], [edall[:, hc, :]])

    DG(xb[:, :, :], x_d.rearrange("(h p) t -> p h t", p=128)[:, :, 0:TS])

    w2flat = w2t_d.rearrange("(f p) d -> p f d", p=128)
    for i in range(HCN):
        sl = w1t_d[i * 128:(i + 1) * 128, :]
        DSW(stg32[:, :], sl)
        V(lambda e, i=i: e.tensor_reduce(absa[:, i:i + 1], stg32[:, :], AXX, ALU.add,
                                         apply_absolute_value=True), [stg32], [absa[:, i:i + 1]])
    V(lambda e: e.tensor_reduce(wsc[:, 0:1], absa[:, 0:HCN], AXX, ALU.add), [absa[:, 0:HCN]], [wsc[:, 0:1]])
    for i in range(HCN):
        DSW(stg32[:, :].rearrange("p (f d) -> p f d", f=4), w2flat[:, 4 * i:4 * i + 4, :])
        V(lambda e, i=i: e.tensor_reduce(absa[:, i:i + 1], stg32[:, :], AXX, ALU.add,
                                         apply_absolute_value=True), [stg32], [absa[:, i:i + 1]])
    V(lambda e: e.tensor_reduce(wsc[:, 1:2], absa[:, 0:HCN], AXX, ALU.add), [absa[:, 0:HCN]], [wsc[:, 1:2]])
    T(lambda e: e.matmul(pst[0][:, 0:2], ones128[:, :], wsc[:, 0:2], start=True, stop=True),
      [ones128, wsc[:, 0:2]], [pst[0][:, 0:2]])
    V(lambda e: e.tensor_scalar(wsc[:, 0:2], pst[0][:, 0:2], 1.0 / (D * FF), None, ALU.mult),
      [pst[0][:, 0:2]], [wsc[:, 0:2]])
    V(lambda e: e.tensor_scalar(wsc[:, 0:2], wsc[:, 0:2], 1e-8, None, ALU.max), [wsc[:, 0:2]], [wsc[:, 0:2]])
    V(lambda e: e.reciprocal(wsc[:, 2:4], wsc[:, 0:2]), [wsc[:, 0:2]], [wsc[:, 2:4]])

    def ternarize(src_ap, inv_ap, width=4096):
        s32 = stg32[:, 0:width]
        sg = stgsc[:, 0:width]
        DSW(s32 if src_ap.ndim == 2 else s32.rearrange("p (f d) -> p f d", f=src_ap.shape[1]), src_ap)
        V(lambda e: e.tensor_scalar(sg, s32, inv_ap, None, ALU.mult), [s32, inv_ap], [sg])
        V(lambda e: e.tensor_scalar(stghi[:, 0:width], sg, 0.5, None, ALU.is_gt), [sg], [stghi[:, 0:width]])
        V(lambda e: e.tensor_scalar(stglo[:, 0:width], sg, -0.5, None, ALU.is_lt), [sg], [stglo[:, 0:width]])
        V(lambda e: e.tensor_tensor(sttrn[:, 0:width], stghi[:, 0:width], stglo[:, 0:width], ALU.subtract),
          [stghi[:, 0:width], stglo[:, 0:width]], [sttrn[:, 0:width]])

    for i in range(HCN):
        ternarize(w1t_d[i * 128:(i + 1) * 128, :], wsc[:, 2:3])
        dst = sw1_d.rearrange("s p d f -> p s d f")[:, :, i, :]
        srcv = sttrn[:, 0:FF].rearrange("p (s f) -> p s f", s=W1SL)
        DGW(dst, srcv)
    for i in range(HCN):
        ternarize(w2flat[:, 4 * i:4 * i + 4, :], wsc[:, 3:4])
        for j in range(4):
            fc = 4 * i + j
            dst = sw2_d.rearrange("(d two) p f i -> p two d f i", two=2)[:, fc // 16, :, fc % 16, :]
            srcv = sttrn[:, j * D:(j + 1) * D].rearrange("p (d i) -> p d i", d=HCN)
            DGW(dst, srcv)
    for pj, wsrc in enumerate((wk_d, wv_d, wr_d, wo_d)):
        wflat = wsrc.rearrange("(k p) d -> p k d", p=128)
        for g in range(2):
            DSW(stg32[:, :].rearrange("p (k d) -> p k d", k=4), wflat[:, 4 * g:4 * g + 4, :])
            V(lambda e: e.tensor_copy(stghi[:, 0:4096], stg32[:, :]), [stg32], [stghi[:, 0:4096]])
            G(lambda e: e.tensor_tensor(stglo[:, 0:4096], stg32[:, :], stghi[:, 0:4096], ALU.subtract),
              [stg32, stghi[:, 0:4096]], [stglo[:, 0:4096]])
            for j in range(4):
                kc = 4 * g + j
                for hlz, srcv in ((0, stghi), (1, stglo)):
                    sv = srcv[:, j * D:(j + 1) * D].rearrange("p (d i) -> p d i", d=HCN)
                    if pj == 2:
                        dst = satt_d[pj].rearrange("d p h k i -> p d h k i")[:, :, hlz, kc, :]
                    else:
                        dst = satt_d[pj].rearrange("k p h d i -> p k h d i")[:, kc, hlz, :, :]
                    DGW(dst, sv)

    x3 = x_d.rearrange("(h p) t -> p h t", p=128)
    y3 = y_d.rearrange("(h p) t -> p h t", p=128)

    def stats_emit(xbuf, hc):
        T(lambda e, hc=hc: e.matmul(pst[0][:, :], ones128[:, :], xbuf[:, hc, :],
                                    start=(hc == 0), stop=(hc == HCN - 1)),
          [ones128, xbuf[:, hc, :]], [pst[0][:, :]])
        Sc(lambda e, hc=hc: e.activation(scr[:, hc, :], xbuf[:, hc, :], AF.Square),
           [xbuf[:, hc, :]], [scr[:, hc, :]])
        T(lambda e, hc=hc: e.matmul(pst[1][:, :], ones128[:, :], scr[:, hc, :],
                                    start=(hc == 0), stop=(hc == HCN - 1)),
          [ones128, scr[:, hc, :]], [pst[1][:, :]])

    mq = stmp[:, 0, :]

    def ln_evac():
        V(lambda e: e.tensor_scalar(mb[:, :], pst[0][:, :], INV_D, None, ALU.mult), [pst[0]], [mb])
        V(lambda e: e.tensor_scalar(mq[:, :], pst[1][:, :], INV_D, None, ALU.mult), [pst[1]], [mq])

    def ln_rest(xbuf, ubuf, split):
        V(lambda e: e.tensor_tensor(vva[:, :], mb[:, :], mb[:, :], ALU.mult), [mb], [vva])
        V(lambda e: e.tensor_tensor(vva[:, :], mq[:, :], vva[:, :], ALU.subtract), [mq, vva], [vva])
        Sc(lambda e: e.activation(mq[:, :], vva[:, :], AF.Ln, bias=cst[:, 0:1]), [vva, cst[:, 0:1]], [mq])
        Sc(lambda e: e.activation(rstd[:, :], mq[:, :], AF.Exp, scale=-0.5), [mq], [rstd])
        for hc in range(5):
            V(lambda e, hc=hc: e.tensor_tensor(ubuf[:, hc, :], xbuf[:, hc, :], mb[:, :], ALU.subtract),
              [xbuf[:, hc, :], mb], [ubuf[:, hc, :]])
        for hc in range(5, HCN):
            G(lambda e, hc=hc: e.tensor_tensor(ubuf[:, hc, :], xbuf[:, hc, :], mb[:, :], ALU.subtract),
              [xbuf[:, hc, :], mb], [ubuf[:, hc, :]])
        for hc in range(5):
            V(lambda e, hc=hc: e.tensor_tensor(ubuf[:, hc, :], ubuf[:, hc, :], rstd[:, :], ALU.mult),
              [ubuf[:, hc, :], rstd], [ubuf[:, hc, :]])
        for hc in range(5, HCN):
            G(lambda e, hc=hc: e.tensor_tensor(ubuf[:, hc, :], ubuf[:, hc, :], rstd[:, :], ALU.mult),
              [ubuf[:, hc, :], rstd], [ubuf[:, hc, :]])
        if split:
            for hc in range(HCN):
                V(lambda e, hc=hc: e.tensor_copy(lnh[:, hc, :], ubuf[:, hc, :]),
                  [ubuf[:, hc, :]], [lnh[:, hc, :]])
                G(lambda e, hc=hc: e.tensor_tensor(lnl[:, hc, :], ubuf[:, hc, :], lnh[:, hc, :],
                                                   ALU.subtract),
                  [ubuf[:, hc, :], lnh[:, hc, :]], [lnl[:, hc, :]])

    def ln_phase(xbuf, ubuf, split, pre_stats=False):
        if not pre_stats:
            for hc in range(HCN):
                stats_emit(xbuf, hc)
        ln_evac()
        ln_rest(xbuf, ubuf, split)

    def w1_preload():
        w1src = sw1_d.rearrange("s p d f -> p s d f")
        for t in range(4):
            DW(t, w1b[:, t, :, :], w1src[:, t, :, :])

    def ffn_h(preloaded=False):
        w1src = sw1_d.rearrange("s p d f -> p s d f")
        if not preloaded:
            w1_preload()
        for fc in range(FCN):
            sl = fc // 2
            j = fc % 2
            bank = pp[fc % 4]
            for dc in range(HCN):
                w_ap = w1b[:, sl % 4, dc, j * 128:(j + 1) * 128]
                for pas, rhsbuf in ((0, lnh), (1, lnl)):
                    T(lambda e, w=w_ap, r=rhsbuf[:, dc, :], b=bank, st=(dc == 0 and pas == 0),
                        sp=(dc == HCN - 1 and pas == 1): e.matmul(b[:, :], w, r, start=st, stop=sp),
                      [w_ap, rhsbuf[:, dc, :]], [bank[:, :]])
            if j == 1 and sl + 4 < W1SL:
                DW(sl, w1b[:, sl % 4, :, :], w1src[:, sl + 4, :, :])
            Sc(lambda e, b=bank, fc=fc: e.activation(hF[:, fc % 2, :], b[:, :], AF.Gelu,
                                                     bias=b1s[:, fc:fc + 1]),
               [bank[:, :], b1s[:, fc:fc + 1]], [hF[:, fc % 2, :]])
            V(lambda e, fc=fc: e.tensor_copy(hh[:, fc, :], hF[:, fc % 2, :]),
              [hF[:, fc % 2, :]], [hh[:, fc, :]])
            G(lambda e, fc=fc: e.tensor_tensor(hl[:, fc, :], hF[:, fc % 2, :], hh[:, fc, :], ALU.subtract),
              [hF[:, fc % 2, :], hh[:, fc, :]], [hl[:, fc, :]])

    def w2_preload():
        w2src = sw2_d.rearrange("s p f i -> p s f i")
        for t in range(4):
            DW(t, w2b[:, t, :, :], w2src[:, t, :, :])

    def ffn_out(xbuf, post_dc=None, preloaded=False):
        w2src = sw2_d.rearrange("s p f i -> p s f i")
        if not preloaded:
            w2_preload()
        for dc in range(HCN):
            bank = po[dc % 2]
            for pas, rhsbuf in ((0, hh), (1, hl)):
                for fc in range(FCN):
                    sl = 2 * dc + fc // 16
                    w_ap = w2b[:, sl % 4, fc % 16, :]
                    T(lambda e, w=w_ap, r=rhsbuf[:, fc, :], b=bank, st=(pas == 0 and fc == 0),
                        sp=(pas == 1 and fc == FCN - 1): e.matmul(b[:, :], w, r, start=st, stop=sp),
                      [w_ap, rhsbuf[:, fc, :]], [bank[:, :]])
            for half in range(2):
                sl = 2 * dc + half
                if sl + 4 < 2 * HCN:
                    DW(sl, w2b[:, sl % 4, :, :], w2src[:, sl + 4, :, :])
            V(lambda e, b=bank, dc=dc: e.tensor_scalar(stmp[:, dc % 2, :], b[:, :], 0.5,
                                                       b2h[:, dc:dc + 1], ALU.mult, ALU.add),
              [bank[:, :], b2h[:, dc:dc + 1]], [stmp[:, dc % 2, :]])
            V(lambda e, dc=dc, x=xbuf: e.tensor_tensor(x[:, dc, :], x[:, dc, :], stmp[:, dc % 2, :], ALU.add),
              [xbuf[:, dc, :], stmp[:, dc % 2, :]], [xbuf[:, dc, :]])
            if post_dc is not None:
                post_dc(dc)

    def mix_stt(mi, buf, ubuf):
        V(lambda e: e.tensor_tensor(buf[:, :, 1:TS], ubuf[:, :, 1:TS], ubuf[:, :, 0:TS - 1], ALU.subtract),
          [ubuf], [buf[:, :, 1:TS]])
        V(lambda e: e.tensor_tensor(buf[:, :, 0:1], ubuf[:, :, 0:1], ucar[:, :, :], ALU.subtract),
          [ubuf[:, :, 0:1], ucar], [buf[:, :, 0:1]])
        for hc in range(HCN):
            V(lambda e, hc=hc: e.scalar_tensor_tensor(buf[:, hc, 1:TS], buf[:, hc, 1:TS],
                                                      tmx[:, mi, hc:hc + 1], ubuf[:, hc, 0:TS - 1],
                                                      ALU.mult, ALU.add),
              [buf[:, hc, 1:TS], tmx[:, mi, hc:hc + 1], ubuf[:, hc, 0:TS - 1]], [buf[:, hc, 1:TS]])
            V(lambda e, hc=hc: e.scalar_tensor_tensor(buf[:, hc, 0:1], buf[:, hc, 0:1],
                                                      tmx[:, mi, hc:hc + 1], ucar[:, hc, :],
                                                      ALU.mult, ALU.add),
              [buf[:, hc, 0:1], tmx[:, mi, hc:hc + 1], ucar[:, hc, :]], [buf[:, hc, 0:1]])

    def mix_copies(buf):
        for hc in range(HCN):
            V(lambda e, hc=hc: e.tensor_copy(lnh[:, hc, :], buf[:, hc, :]),
              [buf[:, hc, :]], [lnh[:, hc, :]])
            G(lambda e, hc=hc: e.tensor_tensor(lnl[:, hc, :], buf[:, hc, :], lnh[:, hc, :],
                                               ALU.subtract),
              [buf[:, hc, :], lnh[:, hc, :]], [lnl[:, hc, :]])

    B8 = pp + po + pst

    def attn_preload(pj):
        asrc = satt_d[pj].rearrange("k p h d i -> p k h d i")
        for t in range(4):
            DW(t, wab[:, t, :, :, :], asrc[:, t, :, :, :])

    def attn_proj_kc8(pj, rhs_pair=None, preloaded=False):
        hi, lo = rhs_pair if rhs_pair is not None else (lnh, lnl)
        asrc = satt_d[pj].rearrange("k p h d i -> p k h d i")
        if not preloaded:
            attn_preload(pj)
        for kc in range(HCN):
            for dc in range(HCN):
                bank = B8[dc]
                for pas, (hlz, rhsbuf) in enumerate(((0, hi), (0, lo), (1, hi))):
                    w_ap = wab[:, kc % 4, hlz, dc, :]
                    T(lambda e, w=w_ap, r=rhsbuf[:, kc, :], b=bank,
                        st=(kc == 0 and pas == 0), sp=(kc == HCN - 1 and pas == 2):
                        e.matmul(b[:, :], w, r, start=st, stop=sp),
                      [w_ap, rhsbuf[:, kc, :]], [bank[:, :]])
            if kc + 4 < HCN:
                DW(kc, wab[:, kc % 4, :, :, :], asrc[:, kc + 4, :, :, :])

    def attn_proj_r(evac):
        asrc = satt_d[2].rearrange("d p h k i -> p d h k i")
        for t in range(4):
            DW(t, wab[:, t, :, :, :], asrc[:, t, :, :, :])
        for dc in range(HCN):
            bank = pp[dc % 4]
            for pas, (hlz, rhsbuf) in enumerate(((0, lnh), (0, lnl), (1, lnh))):
                for kc in range(HCN):
                    w_ap = wab[:, dc % 4, hlz, kc, :]
                    T(lambda e, w=w_ap, r=rhsbuf[:, kc, :], b=bank, st=(pas == 0 and kc == 0),
                        sp=(pas == 2 and kc == HCN - 1): e.matmul(b[:, :], w, r, start=st, stop=sp),
                      [w_ap, rhsbuf[:, kc, :]], [bank[:, :]])
            if dc + 4 < HCN:
                DW(dc, wab[:, dc % 4, :, :, :], asrc[:, dc + 4, :, :, :])
            evac(dc, bank)

    XBUF = [xb, xb2]

    def emit_chunk(c):
        xc = XBUF[c % 2]
        xn = XBUF[(c + 1) % 2]
        ub = SLOT[3]
        w2_preload()
        ffn_h(preloaded=(c > 0))
        attn_preload(0)
        ffn_out(xc, post_dc=lambda dc: stats_emit(xc, dc), preloaded=True)
        ln_evac()
        ln_rest(xc, ub, split=False)
        mix_stt(0, SLOT[2], ub)
        mix_copies(SLOT[2])
        attn_proj_kc8(0, preloaded=True)
        mix_stt(1, scr, ub)
        mix_copies(scr)
        for dc in range(HCN):
            V(lambda e, dc=dc: e.tensor_copy(SLOT[0][:, dc, :], B8[dc][:, :]),
              [B8[dc][:, :]], [SLOT[0][:, dc, :]])
        attn_proj_kc8(1)
        mix_stt(2, SLOT[1], ub)
        V(lambda e: e.tensor_copy(ucar[:, :, :], ub[:, :, TS - 1:TS]), [ub[:, :, TS - 1:TS]], [ucar])
        V(lambda e: e.tensor_reduce(kmax[:, :].unsqueeze(2), SLOT[0][:, :, :], AXX, ALU.max),
          [SLOT[0]], [kmax])
        V(lambda e: e.tensor_tensor(Mx[:, :], kmax[:, :], Mc[:, :], ALU.max), [kmax, Mc], [Mx])
        V(lambda e: e.tensor_scalar(negM[:, :], Mx[:, :], -1.0, None, ALU.mult), [Mx], [negM])
        V(lambda e: e.tensor_tensor(cft[:, :], Mc[:, :], Mx[:, :], ALU.subtract), [Mc, Mx], [cft])
        Sc(lambda e: e.activation(cft[:, :], cft[:, :], AF.Exp), [cft], [cft])
        V(lambda e: e.tensor_copy(Mc[:, :], Mx[:, :]), [Mx], [Mc])
        V(lambda e: e.tensor_tensor(N0[:, :], SNc[:, :], cft[:, :], ALU.mult), [SNc, cft], [N0])
        V(lambda e: e.tensor_tensor(D0[:, :], SDc[:, :], cft[:, :], ALU.mult), [SDc, cft], [D0])
        for hc in range(HCN):
            Sc(lambda e, hc=hc: e.activation(scr[:, hc, :], SLOT[0][:, hc, :], AF.Exp,
                                             bias=negM[:, hc:hc + 1]),
               [SLOT[0][:, hc, :], negM[:, hc:hc + 1]], [scr[:, hc, :]])
        for hc in range(HCN):
            V(lambda e, hc=hc: e.tensor_scalar(SLOT[0][:, hc, :], scr[:, hc, :], etf[:, hc:hc + 1],
                                               None, ALU.mult),
              [scr[:, hc, :], etf[:, hc:hc + 1]], [SLOT[0][:, hc, :]])
        mix_copies(SLOT[1])
        for dc in range(HCN):
            V(lambda e, dc=dc: e.tensor_copy(SLOT[1][:, dc, :], B8[dc][:, :]),
              [B8[dc][:, :]], [SLOT[1][:, dc, :]])
        def evac_r(dc, bank):
            Sc(lambda e, dc=dc, b=bank: e.activation(SLOT[2][:, dc, :], b[:, :], AF.Sigmoid),
               [bank[:, :]], [SLOT[2][:, dc, :]])
        attn_proj_r(evac_r)
        for hc in range(HCN):
            G(lambda e, hc=hc: e.tensor_tensor(SLOT[3][:, hc, :], scr[:, hc, :], SLOT[1][:, hc, :],
                                               ALU.mult),
              [scr[:, hc, :], SLOT[1][:, hc, :]], [SLOT[3][:, hc, :]])
            G(lambda e, hc=hc: e.tensor_tensor(SLOT[1][:, hc, :], SLOT[0][:, hc, :], SLOT[1][:, hc, :],
                                               ALU.mult),
              [SLOT[0][:, hc, :], SLOT[1][:, hc, :]], [SLOT[1][:, hc, :]])
        for hc in range(HCN):
            V(lambda e, hc=hc: e.tensor_tensor_scan(SLOT[3][:, hc, :], edall[:, hc, :],
                                                    SLOT[3][:, hc, :], N0[:, hc:hc + 1],
                                                    ALU.mult, ALU.add),
              [edall[:, hc, :], SLOT[3][:, hc, :], N0[:, hc:hc + 1]], [SLOT[3][:, hc, :]])
            V(lambda e, hc=hc: e.tensor_tensor_scan(scr[:, hc, :], edall[:, hc, :], scr[:, hc, :],
                                                    D0[:, hc:hc + 1], ALU.mult, ALU.add),
              [edall[:, hc, :], scr[:, hc, :], D0[:, hc:hc + 1]], [scr[:, hc, :]])
            V(lambda e, hc=hc: e.tensor_copy(SNc[:, hc:hc + 1], SLOT[3][:, hc, TS - 1:TS]),
              [SLOT[3][:, hc, TS - 1:TS]], [SNc[:, hc:hc + 1]])
            V(lambda e, hc=hc: e.tensor_copy(SDc[:, hc:hc + 1], scr[:, hc, TS - 1:TS]),
              [scr[:, hc, TS - 1:TS]], [SDc[:, hc:hc + 1]])
            G(lambda e, hc=hc: e.tensor_tensor(SLOT[1][:, hc, 1:TS], SLOT[1][:, hc, 1:TS],
                                               SLOT[3][:, hc, 0:TS - 1], ALU.add),
              [SLOT[1][:, hc, 1:TS], SLOT[3][:, hc, 0:TS - 1]], [SLOT[1][:, hc, 1:TS]])
            V(lambda e, hc=hc: e.tensor_tensor(SLOT[1][:, hc, 0:1], SLOT[1][:, hc, 0:1],
                                               N0[:, hc:hc + 1], ALU.add),
              [SLOT[1][:, hc, 0:1], N0[:, hc:hc + 1]], [SLOT[1][:, hc, 0:1]])
            G(lambda e, hc=hc: e.tensor_tensor(SLOT[0][:, hc, 1:TS], SLOT[0][:, hc, 1:TS],
                                               scr[:, hc, 0:TS - 1], ALU.add),
              [SLOT[0][:, hc, 1:TS], scr[:, hc, 0:TS - 1]], [SLOT[0][:, hc, 1:TS]])
            V(lambda e, hc=hc: e.tensor_tensor(SLOT[0][:, hc, 0:1], SLOT[0][:, hc, 0:1],
                                               D0[:, hc:hc + 1], ALU.add),
              [SLOT[0][:, hc, 0:1], D0[:, hc:hc + 1]], [SLOT[0][:, hc, 0:1]])
            Sc(lambda e, hc=hc: e.activation(SLOT[3][:, hc, :], SLOT[0][:, hc, :], AF.Ln),
               [SLOT[0][:, hc, :]], [SLOT[3][:, hc, :]])
            Sc(lambda e, hc=hc: e.activation(SLOT[0][:, hc, :], SLOT[3][:, hc, :], AF.Exp, scale=-1.0),
               [SLOT[3][:, hc, :]], [SLOT[0][:, hc, :]])
            V(lambda e, hc=hc: e.tensor_tensor(SLOT[1][:, hc, :], SLOT[1][:, hc, :], SLOT[0][:, hc, :],
                                               ALU.mult),
              [SLOT[1][:, hc, :], SLOT[0][:, hc, :]], [SLOT[1][:, hc, :]])
            V(lambda e, hc=hc: e.tensor_tensor(SLOT[1][:, hc, :], SLOT[1][:, hc, :], SLOT[2][:, hc, :],
                                               ALU.mult),
              [SLOT[1][:, hc, :], SLOT[2][:, hc, :]], [SLOT[1][:, hc, :]])
            V(lambda e, hc=hc: e.tensor_copy(oh[:, hc, :], SLOT[1][:, hc, :]),
              [SLOT[1][:, hc, :]], [oh[:, hc, :]])
            G(lambda e, hc=hc: e.tensor_tensor(ol[:, hc, :], SLOT[1][:, hc, :], oh[:, hc, :],
                                               ALU.subtract),
              [SLOT[1][:, hc, :], oh[:, hc, :]], [ol[:, hc, :]])
        if c + 1 < nch:
            DG(xn[:, :, :], x3[:, :, (c + 1) * TS:(c + 2) * TS])
        attn_proj_kc8(3, rhs_pair=(oh, ol))
        for dc in range(HCN):
            V(lambda e, dc=dc, x=xc: e.tensor_tensor(x[:, dc, :], x[:, dc, :], B8[dc][:, :], ALU.add),
              [xc[:, dc, :], B8[dc][:, :]], [xc[:, dc, :]])
        for hc in range(HCN):
            stats_emit(xc, hc)
        ln_evac()
        if c + 1 < nch:
            for hc in range(HCN):
                stats_emit(xn, hc)
        ln_rest(xc, scr, split=True)
        ffn_h()
        if c + 1 < nch:
            ln_evac()
            ln_rest(xn, scr, split=True)
        ffn_out(xc)
        if c + 1 < nch:
            w1_preload()
        DG(y3[:, :, c * TS:(c + 1) * TS], xc[:, :, :])

    ln_phase(xb, scr, split=True)
    for c in range(nch):
        emit_chunk(c)
    S.op("sync", lambda e: e.nop(), [y_d[:, :]], [])
    assert nc.sbuf_bytes_remaining >= 0, "SBUF over budget"

    prog = S.emit(nc, sems)
    with nc.Block() as block:
        @block.sync
        def _(eng):
            for f in prog["sync"]:
                f(eng)

        @block.scalar
        def _(eng):
            for f in prog["scalar"]:
                f(eng)

        @block.vector
        def _(eng):
            for f in prog["vector"]:
                f(eng)

        @block.tensor
        def _(eng):
            for f in prog["tensor"]:
                f(eng)

        @block.gpsimd
        def _(eng):
            for f in prog["gpsimd"]:
                f(eng)
    return nc


def make_inputs(inputs, n_tok=T_TOK):
    hid = np.asarray(inputs["L0_attn_hidden"])
    B = hid.shape[0]
    shared = dict(
        wk=np.ascontiguousarray(inputs["L0_attn_Wk"]),
        wv=np.ascontiguousarray(inputs["L0_attn_Wv"]),
        wr=np.ascontiguousarray(inputs["L0_attn_Wr"]),
        wo=np.ascontiguousarray(inputs["L0_attn_Wo"]),
        w1t=np.ascontiguousarray(np.asarray(inputs["L0_ffn_W1"]).T),
        w2t=np.ascontiguousarray(np.asarray(inputs["L0_ffn_W2"]).T),
        b1v=np.ascontiguousarray(np.asarray(inputs["L0_ffn_b1"]).reshape(FCN, 128).T),
        b2v=np.ascontiguousarray(np.asarray(inputs["L0_ffn_b2"]).reshape(HCN, 128).T),
        tdv=np.ascontiguousarray(np.asarray(inputs["L0_attn_time_decay"]).reshape(HCN, 128).T),
        tfv=np.ascontiguousarray(np.asarray(inputs["L0_attn_time_first"]).reshape(HCN, 128).T),
        tmkv=np.ascontiguousarray(np.asarray(inputs["L0_attn_time_mix_key"]).reshape(HCN, 128).T),
        tmvv=np.ascontiguousarray(np.asarray(inputs["L0_attn_time_mix_value"]).reshape(HCN, 128).T),
        tmrv=np.ascontiguousarray(np.asarray(inputs["L0_attn_time_mix_receptance"]).reshape(HCN, 128).T),
    )
    in_maps = []
    for c in range(B):
        m = dict(shared)
        m["x"] = np.ascontiguousarray(hid[c, :n_tok, :].T)
        in_maps.append(m)
    return in_maps


def kernel(**inputs):
    in_maps = make_inputs(inputs)
    nc = build_nc()
    res = run_bass_kernel_spmd(nc, in_maps, core_ids=list(range(8)))
    outs = [np.ascontiguousarray(np.asarray(r["y"]).T) for r in res.results]
    return np.stack(outs, axis=0)


# version 28
# speedup vs baseline: 1.1331x; 1.1331x over previous
import numpy as np
import concourse.bass as bass
import concourse.mybir as mybir
from concourse.bass_utils import run_bass_kernel_spmd

F32 = mybir.dt.float32
BF16 = mybir.dt.bfloat16
AF = mybir.ActivationFunctionType
ALU = mybir.AluOpType
AXX = mybir.AxisListType.X

D = 1024
FF = 4096
HCN = 8
FCN = 32
TS = 512
T_TOK = 8192
NCH = T_TOK // TS
NEG_INIT = -1e5
EPS = 1e-5
W1SL = 16
INV_D = 1.0 / D

ENGINES = ("sync", "scalar", "vector", "tensor", "gpsimd")
DMA_SEM = {"sync": "WD", "gpsimd": "XD", "scalar": "AD"}
ENG_SEM = {"tensor": "TEN", "vector": "VEC", "scalar": "ACT", "gpsimd": "GPS", "sync": "SYN"}


def _rng(ap):
    dts = mybir.dt.size(ap.dtype)
    aps = [list(p) for p in ap.ap]
    try:
        is_dram = str(ap.space) == "MemorySpace.DRAM"
    except Exception:
        is_dram = False
    dims = aps if is_dram else aps[1:]
    span = (sum(abs(s) * (c - 1) for s, c in dims) + 1) if dims else 1
    lo = ap.offset * dts
    return (ap.tensor.name, lo, lo + span * dts)


class _Op:
    __slots__ = ("eng", "fn", "reads", "writes", "dma", "qsem", "deps", "pub", "val", "idx")

    def __init__(self, eng, fn, reads, writes, dma, idx, qsem=None):
        self.eng, self.fn = eng, fn
        self.reads = [_rng(a) for a in reads]
        self.writes = [_rng(a) for a in writes]
        self.dma = dma
        self.qsem = qsem
        self.deps = []
        self.pub = dma
        self.val = None
        self.idx = idx


class Sched:
    def __init__(self):
        self.ops = []
        self.mem = {}
        self.last_dma = {}
        self.n_stream = {}

    def op(self, eng, fn, reads=(), writes=(), dma=False, stream=False):
        qsem = None
        if dma:
            if stream:
                k = self.n_stream.setdefault(eng, 0)
                self.n_stream[eng] = k + 1
                qsem = f"W{({'sync': 0, 'scalar': 2, 'gpsimd': 4}[eng]) + k % 2}"
            else:
                qsem = DMA_SEM[eng]
        o = _Op(eng, fn, reads, writes, dma, len(self.ops), qsem)
        self.ops.append(o)
        dep = set()
        if dma:
            hist = self.last_dma.setdefault((eng, qsem), [])
            if hist:
                dep.add(hist[-1])
            hist.append(o)

        def scan_name(name, lo, hi, want_writers, want_readers):
            ent = self.mem.get(name)
            if not ent:
                return
            for (l2, h2), (wr, rds) in ent.items():
                if l2 < hi and lo < h2:
                    if want_writers and wr is not None:
                        dep.add(wr)
                    if want_readers:
                        for r in rds:
                            dep.add(r)

        for (name, lo, hi) in o.reads:
            scan_name(name, lo, hi, True, False)
        for (name, lo, hi) in o.writes:
            scan_name(name, lo, hi, True, True)
        for (name, lo, hi) in o.reads:
            ent = self.mem.setdefault(name, {})
            ent.setdefault((lo, hi), [None, []])[1].append(o)
        for (name, lo, hi) in o.writes:
            ent = self.mem.setdefault(name, {})
            e = ent.get((lo, hi))
            if e is not None:
                e[0], e[1] = o, []
            else:
                ent[(lo, hi)] = [o, []]
        for p in dep:
            if p is o:
                continue
            if p.eng == o.eng and not p.dma and not o.dma:
                continue
            o.deps.append(p)
        return o

    def emit(self, nc, sems):
        for o in self.ops:
            if not o.deps:
                continue
            best = {}
            for p in o.deps:
                key = p.qsem if p.dma else ENG_SEM[p.eng]
                q = best.get(key)
                if q is None or p.idx > q.idx:
                    best[key] = p
            o.deps = list(best.values())
            for p in o.deps:
                p.pub = True
        cnt = {k: 0 for k in sems}
        for o in self.ops:
            if o.dma:
                o.pub = True
            if o.pub:
                s = o.qsem if o.dma else ENG_SEM[o.eng]
                cnt[s] += 16 if o.dma else 1
                o.val = (s, cnt[s])
        prog = {e: [] for e in ENGINES}
        water = {e: {} for e in ENGINES}
        dirty_w = {e: [] for e in ENGINES}
        dirty_r = {e: [] for e in ENGINES}

        def overlap(r, ds):
            for (n2, l2, h2) in ds:
                if r[0] == n2 and r[1] < h2 and l2 < r[2]:
                    return True
            return False

        for o in self.ops:
            e = o.eng
            for p in sorted(o.deps, key=lambda q: q.idx):
                s, v = p.val
                if water[e].get(s, 0) < v:
                    water[e][s] = v
                    prog[e].append(lambda eng, s=s, v=v: eng.wait_ge(sems[s], v))
            if e in ("vector", "scalar", "gpsimd") and not o.dma:
                need = (any(overlap(r, dirty_w[e]) for r in o.reads)
                        or any(overlap(w, dirty_w[e]) or overlap(w, dirty_r[e]) for w in o.writes))
                if need:
                    prog[e].append(lambda eng: eng.drain())
                    dirty_w[e] = []
                    dirty_r[e] = []
                dirty_w[e].extend(o.writes)
                dirty_r[e].extend(o.reads)
            if o.val is not None:
                s, _ = o.val
                k = 16 if o.dma else 1
                prog[e].append(lambda eng, fn=o.fn, s=s, k=k: fn(eng).then_inc(sems[s], k))
            else:
                prog[e].append(lambda eng, fn=o.fn: fn(eng))
        return prog


def build_nc(n_tok=T_TOK):
    nch = n_tok // TS
    nc = bass.Bass("TRN2", target_bir_lowering=False)

    DT = lambda name, shape, dt=F32, kind="ExternalInput": nc.dram_tensor(name, shape, dt, kind=kind).ap()
    x_d = DT("x", [D, n_tok])
    y_d = DT("y", [D, n_tok], kind="ExternalOutput")
    wk_d = DT("wk", [D, D]); wv_d = DT("wv", [D, D]); wr_d = DT("wr", [D, D]); wo_d = DT("wo", [D, D])
    w1t_d = DT("w1t", [D, FF])
    w2t_d = DT("w2t", [FF, D])
    b1_d = DT("b1v", [128, FCN])
    b2_d = DT("b2v", [128, HCN])
    td_d = DT("tdv", [128, HCN]); tf_d = DT("tfv", [128, HCN])
    tmk_d = DT("tmkv", [128, HCN]); tmv_d = DT("tmvv", [128, HCN]); tmr_d = DT("tmrv", [128, HCN])
    sw1_d = nc.dram_tensor("sw1", [W1SL, 128, HCN, FF // W1SL], BF16, kind="Internal").ap()
    sw2_d = nc.dram_tensor("sw2", [2 * HCN, 128, FCN // 2, 128], BF16, kind="Internal").ap()
    satt_d = nc.dram_tensor("satt", [4, HCN, 128, 2, HCN, 128], BF16, kind="Internal").ap()

    SB = lambda name, shape, dt=F32: nc.alloc_sbuf_tensor(name, shape, dt).ap()
    xb = SB("xb", [128, HCN, TS])
    xb2 = SB("xb2", [128, HCN, TS])
    scr = SB("scr", [128, HCN, TS])
    lnh = SB("lnh", [128, HCN, TS], BF16)
    lnl = SB("lnl", [128, HCN, TS], BF16)
    arena = SB("arena", [128, HCN, 4 * TS])
    SLOT = [arena[:, :, i * TS:(i + 1) * TS] for i in range(4)]
    aflat = arena.rearrange("p h t -> p (h t)")
    aren_bf = aflat.bitcast(BF16)
    hhl = aren_bf.rearrange("p (n t) -> p n t", t=TS)
    hh = hhl[:, 0:FCN, :]
    hl = hhl[:, FCN:2 * FCN, :]
    ohb = SLOT[0].bitcast(BF16)
    oh = ohb[:, :, 0:TS]
    ol = ohb[:, :, TS:2 * TS]
    stg32 = aflat[:, 0:4096]
    stgsc = aflat[:, 4096:8192]
    stghi = aren_bf[:, 16384:20480]
    stglo = aren_bf[:, 20480:24576]
    sttrn = aren_bf[:, 24576:28672]

    edall = SB("edall", [128, HCN, TS])
    mb = SB("mb", [128, TS]); vva = SB("vva", [128, TS])
    rstd = vva
    hF = SB("hF", [128, 2, TS])
    stmp = SB("stmp", [128, 2, TS])
    w1b = SB("w1b", [128, 4, HCN, FF // W1SL], BF16)
    w2b = SB("w2b", [128, 4, FCN // 2, 128], BF16)
    wab = SB("wab", [128, 4, 2, HCN, 128], BF16)
    ones128 = SB("ones128", [128, 128])
    b1s = SB("b1s", [128, FCN])
    b2h = SB("b2h", [128, HCN])
    tmx = SB("tmx", [128, 3, HCN])
    etf = SB("etf", [128, HCN])
    ed8 = SB("ed8", [128, HCN])
    tdn = SB("tdn", [128, HCN])
    SNc = SB("SNc", [128, HCN]); SDc = SB("SDc", [128, HCN]); Mc = SB("Mc", [128, HCN])
    kmax = SB("kmax", [128, HCN]); Mx = SB("Mx", [128, HCN]); negM = SB("negM", [128, HCN])
    cft = SB("cft", [128, HCN]); N0 = SB("N0", [128, HCN]); D0 = SB("D0", [128, HCN])
    ucar = SB("ucar", [128, HCN, 1])
    wsc = SB("wsc", [128, 4])
    cst = SB("cst", [128, 2])
    absa = SB("absa", [128, FCN])

    PS = lambda name: nc.alloc_psum_tensor(name, [128, TS], F32).ap()
    pp = [PS("pp0"), PS("pp1"), PS("pp2"), PS("pp3")]
    po = [PS("po0"), PS("po1")]
    pst = [PS("pst0"), PS("pst1")]

    sems = {k: nc.alloc_semaphore(k) for k in ("WD", "XD", "AD", "W0", "W1", "W2", "W3", "W4", "W5", "TEN", "VEC", "ACT", "GPS", "SYN")}

    S = Sched()
    V = lambda fn, r, w: S.op("vector", fn, r, w)
    Sc = lambda fn, r, w: S.op("scalar", fn, r, w)
    T = lambda fn, r, w: S.op("tensor", fn, r, w)
    G = lambda fn, r, w: S.op("gpsimd", fn, r, w)
    DS = lambda dst, src: S.op("sync", lambda e, d=dst, s=src: e.dma_start(out=d, in_=s), [src], [dst], dma=True)
    DSW = lambda dst, src: S.op("sync", lambda e, d=dst, s=src: e.dma_start(out=d, in_=s), [src], [dst], dma=True, stream=True)
    DGW = lambda dst, src: S.op("gpsimd", lambda e, d=dst, s=src: e.dma_start(out=d, in_=s), [src], [dst], dma=True, stream=True)
    DG = lambda dst, src: S.op("gpsimd", lambda e, d=dst, s=src: e.dma_start(out=d, in_=s), [src], [dst], dma=True)
    DA = lambda dst, src: S.op("scalar", lambda e, d=dst, s=src: e.dma_start(out=d, in_=s), [src], [dst], dma=True)
    DAS = lambda dst, src: S.op("scalar", lambda e, d=dst, s=src: e.dma_start(out=d, in_=s), [src], [dst], dma=True, stream=True)
    DW = lambda idx, dst, src: DSW(dst, src)
    DWA = lambda t, dst, src: (DGW if t % 2 == 0 else DAS)(dst, src)
    VG = lambda i, fn, r, w: (V if (i % HCN) < 5 else G)(fn, r, w)

    DG(b1s[:, :], b1_d[:, :])
    DG(b2h[:, :], b2_d[:, :])
    DG(tmx[:, 0, :], tmk_d[:, :])
    DG(tmx[:, 1, :], tmv_d[:, :])
    DG(tmx[:, 2, :], tmr_d[:, :])
    DG(etf[:, :], tf_d[:, :])
    DG(ed8[:, :], td_d[:, :])
    V(lambda e: e.memset(ones128[:, :], 1.0), [], [ones128])
    V(lambda e: e.memset(SNc[:, :], 0.0), [], [SNc])
    V(lambda e: e.memset(SDc[:, :], 0.0), [], [SDc])
    V(lambda e: e.memset(Mc[:, :], NEG_INIT), [], [Mc])
    V(lambda e: e.memset(ucar[:, :, :], 0.0), [], [ucar])
    V(lambda e: e.memset(cst[:, 0:1], EPS), [], [cst[:, 0:1]])
    V(lambda e: e.memset(cst[:, 1:2], 1.0), [], [cst[:, 1:2]])
    V(lambda e: e.tensor_scalar(b2h[:, :], b2h[:, :], 0.5, None, ALU.mult), [b2h], [b2h])
    Sc(lambda e: e.activation(etf[:, :], etf[:, :], AF.Exp), [etf], [etf])
    Sc(lambda e: e.activation(tdn[:, :], ed8[:, :], AF.Exp), [ed8], [tdn])
    V(lambda e: e.tensor_scalar(tdn[:, :], tdn[:, :], -1.0, None, ALU.mult), [tdn], [tdn])
    V(lambda e: e.memset(mb[:, :], 0.0), [], [mb])
    for hc in range(HCN):
        Sc(lambda e, hc=hc: e.activation(edall[:, hc, :], mb[:, :], AF.Exp, bias=tdn[:, hc:hc + 1],
                                         scale=0.0),
           [mb, tdn[:, hc:hc + 1]], [edall[:, hc, :]])

    DG(xb[:, :, :], x_d.rearrange("(h p) t -> p h t", p=128)[:, :, 0:TS])

    w2flat = w2t_d.rearrange("(f p) d -> p f d", p=128)
    for i in range(HCN):
        sl = w1t_d[i * 128:(i + 1) * 128, :]
        DSW(stg32[:, :], sl)
        V(lambda e, i=i: e.tensor_reduce(absa[:, i:i + 1], stg32[:, :], AXX, ALU.add,
                                         apply_absolute_value=True), [stg32], [absa[:, i:i + 1]])
    V(lambda e: e.tensor_reduce(wsc[:, 0:1], absa[:, 0:HCN], AXX, ALU.add), [absa[:, 0:HCN]], [wsc[:, 0:1]])
    for i in range(HCN):
        DSW(stg32[:, :].rearrange("p (f d) -> p f d", f=4), w2flat[:, 4 * i:4 * i + 4, :])
        V(lambda e, i=i: e.tensor_reduce(absa[:, i:i + 1], stg32[:, :], AXX, ALU.add,
                                         apply_absolute_value=True), [stg32], [absa[:, i:i + 1]])
    V(lambda e: e.tensor_reduce(wsc[:, 1:2], absa[:, 0:HCN], AXX, ALU.add), [absa[:, 0:HCN]], [wsc[:, 1:2]])
    T(lambda e: e.matmul(pst[0][:, 0:2], ones128[:, :], wsc[:, 0:2], start=True, stop=True),
      [ones128, wsc[:, 0:2]], [pst[0][:, 0:2]])
    V(lambda e: e.tensor_scalar(wsc[:, 0:2], pst[0][:, 0:2], 1.0 / (D * FF), None, ALU.mult),
      [pst[0][:, 0:2]], [wsc[:, 0:2]])
    V(lambda e: e.tensor_scalar(wsc[:, 0:2], wsc[:, 0:2], 1e-8, None, ALU.max), [wsc[:, 0:2]], [wsc[:, 0:2]])
    V(lambda e: e.reciprocal(wsc[:, 2:4], wsc[:, 0:2]), [wsc[:, 0:2]], [wsc[:, 2:4]])

    def ternarize(src_ap, inv_ap, width=4096):
        s32 = stg32[:, 0:width]
        sg = stgsc[:, 0:width]
        DSW(s32 if src_ap.ndim == 2 else s32.rearrange("p (f d) -> p f d", f=src_ap.shape[1]), src_ap)
        V(lambda e: e.tensor_scalar(sg, s32, inv_ap, None, ALU.mult), [s32, inv_ap], [sg])
        V(lambda e: e.tensor_scalar(stghi[:, 0:width], sg, 0.5, None, ALU.is_gt), [sg], [stghi[:, 0:width]])
        V(lambda e: e.tensor_scalar(stglo[:, 0:width], sg, -0.5, None, ALU.is_lt), [sg], [stglo[:, 0:width]])
        V(lambda e: e.tensor_tensor(sttrn[:, 0:width], stghi[:, 0:width], stglo[:, 0:width], ALU.subtract),
          [stghi[:, 0:width], stglo[:, 0:width]], [sttrn[:, 0:width]])

    for i in range(HCN):
        ternarize(w1t_d[i * 128:(i + 1) * 128, :], wsc[:, 2:3])
        dst = sw1_d.rearrange("s p d f -> p s d f")[:, :, i, :]
        srcv = sttrn[:, 0:FF].rearrange("p (s f) -> p s f", s=W1SL)
        DGW(dst, srcv)
    for i in range(HCN):
        ternarize(w2flat[:, 4 * i:4 * i + 4, :], wsc[:, 3:4])
        for j in range(4):
            fc = 4 * i + j
            dst = sw2_d.rearrange("(d two) p f i -> p two d f i", two=2)[:, fc // 16, :, fc % 16, :]
            srcv = sttrn[:, j * D:(j + 1) * D].rearrange("p (d i) -> p d i", d=HCN)
            DGW(dst, srcv)
    for pj, wsrc in enumerate((wk_d, wv_d, wr_d, wo_d)):
        wflat = wsrc.rearrange("(k p) d -> p k d", p=128)
        for g in range(2):
            DSW(stg32[:, :].rearrange("p (k d) -> p k d", k=4), wflat[:, 4 * g:4 * g + 4, :])
            V(lambda e: e.tensor_copy(stghi[:, 0:4096], stg32[:, :]), [stg32], [stghi[:, 0:4096]])
            G(lambda e: e.tensor_tensor(stglo[:, 0:4096], stg32[:, :], stghi[:, 0:4096], ALU.subtract),
              [stg32, stghi[:, 0:4096]], [stglo[:, 0:4096]])
            for j in range(4):
                kc = 4 * g + j
                for hlz, srcv in ((0, stghi), (1, stglo)):
                    sv = srcv[:, j * D:(j + 1) * D].rearrange("p (d i) -> p d i", d=HCN)
                    if pj == 2:
                        dst = satt_d[pj].rearrange("d p h k i -> p d h k i")[:, :, hlz, kc, :]
                    else:
                        dst = satt_d[pj].rearrange("k p h d i -> p k h d i")[:, kc, hlz, :, :]
                    DGW(dst, sv)

    x3 = x_d.rearrange("(h p) t -> p h t", p=128)
    y3 = y_d.rearrange("(h p) t -> p h t", p=128)

    def stats_emit(xbuf, hc):
        T(lambda e, hc=hc: e.matmul(pst[0][:, :], ones128[:, :], xbuf[:, hc, :],
                                    start=(hc == 0), stop=(hc == HCN - 1)),
          [ones128, xbuf[:, hc, :]], [pst[0][:, :]])
        Sc(lambda e, hc=hc: e.activation(scr[:, hc, :], xbuf[:, hc, :], AF.Square),
           [xbuf[:, hc, :]], [scr[:, hc, :]])
        T(lambda e, hc=hc: e.matmul(pst[1][:, :], ones128[:, :], scr[:, hc, :],
                                    start=(hc == 0), stop=(hc == HCN - 1)),
          [ones128, scr[:, hc, :]], [pst[1][:, :]])

    mq = stmp[:, 0, :]

    def ln_evac():
        V(lambda e: e.tensor_scalar(mb[:, :], pst[0][:, :], INV_D, None, ALU.mult), [pst[0]], [mb])
        V(lambda e: e.tensor_scalar(mq[:, :], pst[1][:, :], INV_D, None, ALU.mult), [pst[1]], [mq])

    def ln_rest(xbuf, ubuf, split):
        V(lambda e: e.tensor_tensor(vva[:, :], mb[:, :], mb[:, :], ALU.mult), [mb], [vva])
        V(lambda e: e.tensor_tensor(vva[:, :], mq[:, :], vva[:, :], ALU.subtract), [mq, vva], [vva])
        Sc(lambda e: e.activation(mq[:, :], vva[:, :], AF.Ln, bias=cst[:, 0:1]), [vva, cst[:, 0:1]], [mq])
        Sc(lambda e: e.activation(rstd[:, :], mq[:, :], AF.Exp, scale=-0.5), [mq], [rstd])
        for hc in range(5):
            V(lambda e, hc=hc: e.tensor_tensor(ubuf[:, hc, :], xbuf[:, hc, :], mb[:, :], ALU.subtract),
              [xbuf[:, hc, :], mb], [ubuf[:, hc, :]])
        for hc in range(5, HCN):
            G(lambda e, hc=hc: e.tensor_tensor(ubuf[:, hc, :], xbuf[:, hc, :], mb[:, :], ALU.subtract),
              [xbuf[:, hc, :], mb], [ubuf[:, hc, :]])
        for hc in range(5):
            V(lambda e, hc=hc: e.tensor_tensor(ubuf[:, hc, :], ubuf[:, hc, :], rstd[:, :], ALU.mult),
              [ubuf[:, hc, :], rstd], [ubuf[:, hc, :]])
        for hc in range(5, HCN):
            G(lambda e, hc=hc: e.tensor_tensor(ubuf[:, hc, :], ubuf[:, hc, :], rstd[:, :], ALU.mult),
              [ubuf[:, hc, :], rstd], [ubuf[:, hc, :]])
        if split:
            for hc in range(HCN):
                Sc(lambda e, hc=hc: e.activation(lnh[:, hc, :], ubuf[:, hc, :], AF.Copy),
                   [ubuf[:, hc, :]], [lnh[:, hc, :]])
                G(lambda e, hc=hc: e.tensor_tensor(lnl[:, hc, :], ubuf[:, hc, :], lnh[:, hc, :],
                                                   ALU.subtract),
                  [ubuf[:, hc, :], lnh[:, hc, :]], [lnl[:, hc, :]])

    def ln_phase(xbuf, ubuf, split, pre_stats=False):
        if not pre_stats:
            for hc in range(HCN):
                stats_emit(xbuf, hc)
        ln_evac()
        ln_rest(xbuf, ubuf, split)

    def w1_preload():
        w1src = sw1_d.rearrange("s p d f -> p s d f")
        for t in range(4):
            DW(t, w1b[:, t, :, :], w1src[:, t, :, :])

    def ffn_h(preloaded=False):
        w1src = sw1_d.rearrange("s p d f -> p s d f")
        if not preloaded:
            w1_preload()
        for fc in range(FCN):
            sl = fc // 2
            j = fc % 2
            bank = pp[fc % 4]
            for dc in range(HCN):
                w_ap = w1b[:, sl % 4, dc, j * 128:(j + 1) * 128]
                for pas, rhsbuf in ((0, lnh), (1, lnl)):
                    T(lambda e, w=w_ap, r=rhsbuf[:, dc, :], b=bank, st=(dc == 0 and pas == 0),
                        sp=(dc == HCN - 1 and pas == 1): e.matmul(b[:, :], w, r, start=st, stop=sp),
                      [w_ap, rhsbuf[:, dc, :]], [bank[:, :]])
            if j == 1 and sl + 4 < W1SL:
                DW(sl, w1b[:, sl % 4, :, :], w1src[:, sl + 4, :, :])
            Sc(lambda e, b=bank, fc=fc: e.activation(hF[:, fc % 2, :], b[:, :], AF.Gelu,
                                                     bias=b1s[:, fc:fc + 1]),
               [bank[:, :], b1s[:, fc:fc + 1]], [hF[:, fc % 2, :]])
            V(lambda e, fc=fc: e.tensor_copy(hh[:, fc, :], hF[:, fc % 2, :]),
              [hF[:, fc % 2, :]], [hh[:, fc, :]])
            G(lambda e, fc=fc: e.tensor_tensor(hl[:, fc, :], hF[:, fc % 2, :], hh[:, fc, :], ALU.subtract),
              [hF[:, fc % 2, :], hh[:, fc, :]], [hl[:, fc, :]])

    def w2_preload():
        w2src = sw2_d.rearrange("s p f i -> p s f i")
        for t in range(4):
            DW(t, w2b[:, t, :, :], w2src[:, t, :, :])

    def ffn_out(xbuf, post_dc=None, preloaded=False):
        w2src = sw2_d.rearrange("s p f i -> p s f i")
        if not preloaded:
            w2_preload()
        for dc in range(HCN):
            bank = po[dc % 2]
            for pas, rhsbuf in ((0, hh), (1, hl)):
                for fc in range(FCN):
                    sl = 2 * dc + fc // 16
                    w_ap = w2b[:, sl % 4, fc % 16, :]
                    T(lambda e, w=w_ap, r=rhsbuf[:, fc, :], b=bank, st=(pas == 0 and fc == 0),
                        sp=(pas == 1 and fc == FCN - 1): e.matmul(b[:, :], w, r, start=st, stop=sp),
                      [w_ap, rhsbuf[:, fc, :]], [bank[:, :]])
            for half in range(2):
                sl = 2 * dc + half
                if sl + 4 < 2 * HCN:
                    DW(sl, w2b[:, sl % 4, :, :], w2src[:, sl + 4, :, :])
            V(lambda e, b=bank, dc=dc: e.tensor_scalar(stmp[:, dc % 2, :], b[:, :], 0.5,
                                                       b2h[:, dc:dc + 1], ALU.mult, ALU.add),
              [bank[:, :], b2h[:, dc:dc + 1]], [stmp[:, dc % 2, :]])
            V(lambda e, dc=dc, x=xbuf: e.tensor_tensor(x[:, dc, :], x[:, dc, :], stmp[:, dc % 2, :], ALU.add),
              [xbuf[:, dc, :], stmp[:, dc % 2, :]], [xbuf[:, dc, :]])
            if post_dc is not None:
                post_dc(dc)

    def mix_stt(mi, buf, ubuf):
        V(lambda e: e.tensor_tensor(buf[:, :, 1:TS], ubuf[:, :, 1:TS], ubuf[:, :, 0:TS - 1], ALU.subtract),
          [ubuf], [buf[:, :, 1:TS]])
        V(lambda e: e.tensor_tensor(buf[:, :, 0:1], ubuf[:, :, 0:1], ucar[:, :, :], ALU.subtract),
          [ubuf[:, :, 0:1], ucar], [buf[:, :, 0:1]])
        for hc in range(HCN):
            V(lambda e, hc=hc: e.scalar_tensor_tensor(buf[:, hc, 1:TS], buf[:, hc, 1:TS],
                                                      tmx[:, mi, hc:hc + 1], ubuf[:, hc, 0:TS - 1],
                                                      ALU.mult, ALU.add),
              [buf[:, hc, 1:TS], tmx[:, mi, hc:hc + 1], ubuf[:, hc, 0:TS - 1]], [buf[:, hc, 1:TS]])
            V(lambda e, hc=hc: e.scalar_tensor_tensor(buf[:, hc, 0:1], buf[:, hc, 0:1],
                                                      tmx[:, mi, hc:hc + 1], ucar[:, hc, :],
                                                      ALU.mult, ALU.add),
              [buf[:, hc, 0:1], tmx[:, mi, hc:hc + 1], ucar[:, hc, :]], [buf[:, hc, 0:1]])

    def mix_copies(buf):
        for hc in range(HCN):
            Sc(lambda e, hc=hc: e.activation(lnh[:, hc, :], buf[:, hc, :], AF.Copy),
               [buf[:, hc, :]], [lnh[:, hc, :]])
            G(lambda e, hc=hc: e.tensor_tensor(lnl[:, hc, :], buf[:, hc, :], lnh[:, hc, :],
                                               ALU.subtract),
              [buf[:, hc, :], lnh[:, hc, :]], [lnl[:, hc, :]])

    def ln2_mixk_chain(xbuf, ub, buf):
        V(lambda e: e.tensor_tensor(vva[:, :], mb[:, :], mb[:, :], ALU.mult), [mb], [vva])
        V(lambda e: e.tensor_tensor(vva[:, :], mq[:, :], vva[:, :], ALU.subtract), [mq, vva], [vva])
        Sc(lambda e: e.activation(mq[:, :], vva[:, :], AF.Ln, bias=cst[:, 0:1]), [vva, cst[:, 0:1]], [mq])
        Sc(lambda e: e.activation(rstd[:, :], mq[:, :], AF.Exp, scale=-0.5), [mq], [rstd])
        for hc in range(HCN):
            E = V if hc < 5 else G
            E(lambda e, hc=hc: e.tensor_tensor(ub[:, hc, :], xbuf[:, hc, :], mb[:, :], ALU.subtract),
              [xbuf[:, hc, :], mb], [ub[:, hc, :]])
            E(lambda e, hc=hc: e.tensor_tensor(ub[:, hc, :], ub[:, hc, :], rstd[:, :], ALU.mult),
              [ub[:, hc, :], rstd], [ub[:, hc, :]])
            V(lambda e, hc=hc: e.tensor_tensor(buf[:, hc, 1:TS], ub[:, hc, 1:TS],
                                               ub[:, hc, 0:TS - 1], ALU.subtract),
              [ub[:, hc, :]], [buf[:, hc, 1:TS]])
            V(lambda e, hc=hc: e.tensor_tensor(buf[:, hc, 0:1], ub[:, hc, 0:1], ucar[:, hc, :],
                                               ALU.subtract),
              [ub[:, hc, 0:1], ucar[:, hc, :]], [buf[:, hc, 0:1]])
            V(lambda e, hc=hc: e.scalar_tensor_tensor(buf[:, hc, 1:TS], buf[:, hc, 1:TS],
                                                      tmx[:, 0, hc:hc + 1], ub[:, hc, 0:TS - 1],
                                                      ALU.mult, ALU.add),
              [buf[:, hc, 1:TS], tmx[:, 0, hc:hc + 1], ub[:, hc, 0:TS - 1]], [buf[:, hc, 1:TS]])
            V(lambda e, hc=hc: e.scalar_tensor_tensor(buf[:, hc, 0:1], buf[:, hc, 0:1],
                                                      tmx[:, 0, hc:hc + 1], ucar[:, hc, :],
                                                      ALU.mult, ALU.add),
              [buf[:, hc, 0:1], tmx[:, 0, hc:hc + 1], ucar[:, hc, :]], [buf[:, hc, 0:1]])
            Sc(lambda e, hc=hc: e.activation(lnh[:, hc, :], buf[:, hc, :], AF.Copy),
               [buf[:, hc, :]], [lnh[:, hc, :]])
            G(lambda e, hc=hc: e.tensor_tensor(lnl[:, hc, :], buf[:, hc, :], lnh[:, hc, :],
                                               ALU.subtract),
              [buf[:, hc, :], lnh[:, hc, :]], [lnl[:, hc, :]])

    B8 = pp + po + pst

    def attn_preload(pj):
        asrc = satt_d[pj].rearrange("k p h d i -> p k h d i")
        for t in range(4):
            DW(t, wab[:, t, :, :, :], asrc[:, t, :, :, :])

    def attn_proj_kc8(pj, rhs_pair=None, preloaded=False):
        hi, lo = rhs_pair if rhs_pair is not None else (lnh, lnl)
        asrc = satt_d[pj].rearrange("k p h d i -> p k h d i")
        if not preloaded:
            attn_preload(pj)
        for kc in range(HCN):
            for dc in range(HCN):
                bank = B8[dc]
                for pas, (hlz, rhsbuf) in enumerate(((0, hi), (0, lo), (1, hi))):
                    w_ap = wab[:, kc % 4, hlz, dc, :]
                    T(lambda e, w=w_ap, r=rhsbuf[:, kc, :], b=bank,
                        st=(kc == 0 and pas == 0), sp=(kc == HCN - 1 and pas == 2):
                        e.matmul(b[:, :], w, r, start=st, stop=sp),
                      [w_ap, rhsbuf[:, kc, :]], [bank[:, :]])
            if kc + 4 < HCN:
                DW(kc, wab[:, kc % 4, :, :, :], asrc[:, kc + 4, :, :, :])

    def attn_proj_r(evac):
        asrc = satt_d[2].rearrange("d p h k i -> p d h k i")
        for t in range(4):
            DW(t, wab[:, t, :, :, :], asrc[:, t, :, :, :])
        for dc in range(HCN):
            bank = pp[dc % 4]
            for pas, (hlz, rhsbuf) in enumerate(((0, lnh), (0, lnl), (1, lnh))):
                for kc in range(HCN):
                    w_ap = wab[:, dc % 4, hlz, kc, :]
                    T(lambda e, w=w_ap, r=rhsbuf[:, kc, :], b=bank, st=(pas == 0 and kc == 0),
                        sp=(pas == 2 and kc == HCN - 1): e.matmul(b[:, :], w, r, start=st, stop=sp),
                      [w_ap, rhsbuf[:, kc, :]], [bank[:, :]])
            if dc + 4 < HCN:
                DW(dc, wab[:, dc % 4, :, :, :], asrc[:, dc + 4, :, :, :])
            evac(dc, bank)

    XBUF = [xb, xb2]

    def emit_chunk(c):
        xc = XBUF[c % 2]
        xn = XBUF[(c + 1) % 2]
        ub = SLOT[3]
        w2_preload()
        ffn_h(preloaded=(c > 0))
        attn_preload(0)
        ffn_out(xc, post_dc=lambda dc: stats_emit(xc, dc), preloaded=True)
        ln_evac()
        ln2_mixk_chain(xc, ub, SLOT[2])
        attn_proj_kc8(0, preloaded=True)
        mix_stt(1, scr, ub)
        mix_copies(scr)
        for dc in range(HCN):
            V(lambda e, dc=dc: e.tensor_copy(SLOT[0][:, dc, :], B8[dc][:, :]),
              [B8[dc][:, :]], [SLOT[0][:, dc, :]])
        attn_proj_kc8(1)
        mix_stt(2, SLOT[1], ub)
        V(lambda e: e.tensor_copy(ucar[:, :, :], ub[:, :, TS - 1:TS]), [ub[:, :, TS - 1:TS]], [ucar])
        V(lambda e: e.tensor_reduce(kmax[:, :].unsqueeze(2), SLOT[0][:, :, :], AXX, ALU.max),
          [SLOT[0]], [kmax])
        V(lambda e: e.tensor_tensor(Mx[:, :], kmax[:, :], Mc[:, :], ALU.max), [kmax, Mc], [Mx])
        V(lambda e: e.tensor_scalar(negM[:, :], Mx[:, :], -1.0, None, ALU.mult), [Mx], [negM])
        V(lambda e: e.tensor_tensor(cft[:, :], Mc[:, :], Mx[:, :], ALU.subtract), [Mc, Mx], [cft])
        Sc(lambda e: e.activation(cft[:, :], cft[:, :], AF.Exp), [cft], [cft])
        V(lambda e: e.tensor_copy(Mc[:, :], Mx[:, :]), [Mx], [Mc])
        V(lambda e: e.tensor_tensor(N0[:, :], SNc[:, :], cft[:, :], ALU.mult), [SNc, cft], [N0])
        V(lambda e: e.tensor_tensor(D0[:, :], SDc[:, :], cft[:, :], ALU.mult), [SDc, cft], [D0])
        for hc in range(HCN):
            Sc(lambda e, hc=hc: e.activation(scr[:, hc, :], SLOT[0][:, hc, :], AF.Exp,
                                             bias=negM[:, hc:hc + 1]),
               [SLOT[0][:, hc, :], negM[:, hc:hc + 1]], [scr[:, hc, :]])
        for hc in range(HCN):
            V(lambda e, hc=hc: e.tensor_scalar(SLOT[0][:, hc, :], scr[:, hc, :], etf[:, hc:hc + 1],
                                               None, ALU.mult),
              [scr[:, hc, :], etf[:, hc:hc + 1]], [SLOT[0][:, hc, :]])
        mix_copies(SLOT[1])
        for dc in range(HCN):
            V(lambda e, dc=dc: e.tensor_copy(SLOT[1][:, dc, :], B8[dc][:, :]),
              [B8[dc][:, :]], [SLOT[1][:, dc, :]])
        def evac_r(dc, bank):
            Sc(lambda e, dc=dc, b=bank: e.activation(SLOT[2][:, dc, :], b[:, :], AF.Sigmoid),
               [bank[:, :]], [SLOT[2][:, dc, :]])
        attn_proj_r(evac_r)
        for hc in range(HCN):
            G(lambda e, hc=hc: e.tensor_tensor(SLOT[3][:, hc, :], scr[:, hc, :], SLOT[1][:, hc, :],
                                               ALU.mult),
              [scr[:, hc, :], SLOT[1][:, hc, :]], [SLOT[3][:, hc, :]])
        for hc in range(HCN):
            G(lambda e, hc=hc: e.tensor_tensor(SLOT[1][:, hc, :], SLOT[0][:, hc, :], SLOT[1][:, hc, :],
                                               ALU.mult),
              [SLOT[0][:, hc, :], SLOT[1][:, hc, :]], [SLOT[1][:, hc, :]])
        for hc in range(HCN):
            V(lambda e, hc=hc: e.tensor_tensor_scan(SLOT[3][:, hc, :], edall[:, hc, :],
                                                    SLOT[3][:, hc, :], N0[:, hc:hc + 1],
                                                    ALU.mult, ALU.add),
              [edall[:, hc, :], SLOT[3][:, hc, :], N0[:, hc:hc + 1]], [SLOT[3][:, hc, :]])
            V(lambda e, hc=hc: e.tensor_tensor_scan(scr[:, hc, :], edall[:, hc, :], scr[:, hc, :],
                                                    D0[:, hc:hc + 1], ALU.mult, ALU.add),
              [edall[:, hc, :], scr[:, hc, :], D0[:, hc:hc + 1]], [scr[:, hc, :]])
            V(lambda e, hc=hc: e.tensor_copy(SNc[:, hc:hc + 1], SLOT[3][:, hc, TS - 1:TS]),
              [SLOT[3][:, hc, TS - 1:TS]], [SNc[:, hc:hc + 1]])
            V(lambda e, hc=hc: e.tensor_copy(SDc[:, hc:hc + 1], scr[:, hc, TS - 1:TS]),
              [scr[:, hc, TS - 1:TS]], [SDc[:, hc:hc + 1]])
            G(lambda e, hc=hc: e.tensor_tensor(SLOT[1][:, hc, 1:TS], SLOT[1][:, hc, 1:TS],
                                               SLOT[3][:, hc, 0:TS - 1], ALU.add),
              [SLOT[1][:, hc, 1:TS], SLOT[3][:, hc, 0:TS - 1]], [SLOT[1][:, hc, 1:TS]])
            V(lambda e, hc=hc: e.tensor_tensor(SLOT[1][:, hc, 0:1], SLOT[1][:, hc, 0:1],
                                               N0[:, hc:hc + 1], ALU.add),
              [SLOT[1][:, hc, 0:1], N0[:, hc:hc + 1]], [SLOT[1][:, hc, 0:1]])
            G(lambda e, hc=hc: e.tensor_tensor(SLOT[0][:, hc, 1:TS], SLOT[0][:, hc, 1:TS],
                                               scr[:, hc, 0:TS - 1], ALU.add),
              [SLOT[0][:, hc, 1:TS], scr[:, hc, 0:TS - 1]], [SLOT[0][:, hc, 1:TS]])
            V(lambda e, hc=hc: e.tensor_tensor(SLOT[0][:, hc, 0:1], SLOT[0][:, hc, 0:1],
                                               D0[:, hc:hc + 1], ALU.add),
              [SLOT[0][:, hc, 0:1], D0[:, hc:hc + 1]], [SLOT[0][:, hc, 0:1]])
            Sc(lambda e, hc=hc: e.activation(SLOT[3][:, hc, :], SLOT[0][:, hc, :], AF.Ln),
               [SLOT[0][:, hc, :]], [SLOT[3][:, hc, :]])
            Sc(lambda e, hc=hc: e.activation(SLOT[0][:, hc, :], SLOT[3][:, hc, :], AF.Exp, scale=-1.0),
               [SLOT[3][:, hc, :]], [SLOT[0][:, hc, :]])
            V(lambda e, hc=hc: e.tensor_tensor(SLOT[1][:, hc, :], SLOT[1][:, hc, :], SLOT[0][:, hc, :],
                                               ALU.mult),
              [SLOT[1][:, hc, :], SLOT[0][:, hc, :]], [SLOT[1][:, hc, :]])
            V(lambda e, hc=hc: e.tensor_tensor(SLOT[1][:, hc, :], SLOT[1][:, hc, :], SLOT[2][:, hc, :],
                                               ALU.mult),
              [SLOT[1][:, hc, :], SLOT[2][:, hc, :]], [SLOT[1][:, hc, :]])
            Sc(lambda e, hc=hc: e.activation(oh[:, hc, :], SLOT[1][:, hc, :], AF.Copy),
               [SLOT[1][:, hc, :]], [oh[:, hc, :]])
            G(lambda e, hc=hc: e.tensor_tensor(ol[:, hc, :], SLOT[1][:, hc, :], oh[:, hc, :],
                                               ALU.subtract),
              [SLOT[1][:, hc, :], oh[:, hc, :]], [ol[:, hc, :]])
        if c + 1 < nch:
            DG(xn[:, :, :], x3[:, :, (c + 1) * TS:(c + 2) * TS])
        attn_proj_kc8(3, rhs_pair=(oh, ol))
        for dc in range(HCN):
            V(lambda e, dc=dc, x=xc: e.tensor_tensor(x[:, dc, :], x[:, dc, :], B8[dc][:, :], ALU.add),
              [xc[:, dc, :], B8[dc][:, :]], [xc[:, dc, :]])
        for hc in range(HCN):
            stats_emit(xc, hc)
        ln_evac()
        if c + 1 < nch:
            for hc in range(HCN):
                stats_emit(xn, hc)
        ln_rest(xc, scr, split=True)
        ffn_h()
        if c + 1 < nch:
            ln_evac()
            ln_rest(xn, scr, split=True)
        ffn_out(xc)
        if c + 1 < nch:
            w1_preload()
        DG(y3[:, :, c * TS:(c + 1) * TS], xc[:, :, :])

    ln_phase(xb, scr, split=True)
    for c in range(nch):
        emit_chunk(c)
    S.op("sync", lambda e: e.nop(), [y_d[:, :]], [])
    assert nc.sbuf_bytes_remaining >= 0, "SBUF over budget"

    prog = S.emit(nc, sems)
    with nc.Block() as block:
        @block.sync
        def _(eng):
            for f in prog["sync"]:
                f(eng)

        @block.scalar
        def _(eng):
            for f in prog["scalar"]:
                f(eng)

        @block.vector
        def _(eng):
            for f in prog["vector"]:
                f(eng)

        @block.tensor
        def _(eng):
            for f in prog["tensor"]:
                f(eng)

        @block.gpsimd
        def _(eng):
            for f in prog["gpsimd"]:
                f(eng)
    return nc


def make_inputs(inputs, n_tok=T_TOK):
    hid = np.asarray(inputs["L0_attn_hidden"])
    B = hid.shape[0]
    shared = dict(
        wk=np.ascontiguousarray(inputs["L0_attn_Wk"]),
        wv=np.ascontiguousarray(inputs["L0_attn_Wv"]),
        wr=np.ascontiguousarray(inputs["L0_attn_Wr"]),
        wo=np.ascontiguousarray(inputs["L0_attn_Wo"]),
        w1t=np.ascontiguousarray(np.asarray(inputs["L0_ffn_W1"]).T),
        w2t=np.ascontiguousarray(np.asarray(inputs["L0_ffn_W2"]).T),
        b1v=np.ascontiguousarray(np.asarray(inputs["L0_ffn_b1"]).reshape(FCN, 128).T),
        b2v=np.ascontiguousarray(np.asarray(inputs["L0_ffn_b2"]).reshape(HCN, 128).T),
        tdv=np.ascontiguousarray(np.asarray(inputs["L0_attn_time_decay"]).reshape(HCN, 128).T),
        tfv=np.ascontiguousarray(np.asarray(inputs["L0_attn_time_first"]).reshape(HCN, 128).T),
        tmkv=np.ascontiguousarray(np.asarray(inputs["L0_attn_time_mix_key"]).reshape(HCN, 128).T),
        tmvv=np.ascontiguousarray(np.asarray(inputs["L0_attn_time_mix_value"]).reshape(HCN, 128).T),
        tmrv=np.ascontiguousarray(np.asarray(inputs["L0_attn_time_mix_receptance"]).reshape(HCN, 128).T),
    )
    in_maps = []
    for c in range(B):
        m = dict(shared)
        m["x"] = np.ascontiguousarray(hid[c, :n_tok, :].T)
        in_maps.append(m)
    return in_maps


def kernel(**inputs):
    in_maps = make_inputs(inputs)
    nc = build_nc()
    res = run_bass_kernel_spmd(nc, in_maps, core_ids=list(range(8)))
    outs = [np.ascontiguousarray(np.asarray(r["y"]).T) for r in res.results]
    return np.stack(outs, axis=0)
